# version 05
# baseline (speedup 1.0000x reference)
import os
import numpy as np
import ml_dtypes
import concourse.bass as bass
import concourse.mybir as mybir
from concourse.bass_utils import run_bass_kernel_spmd
from concourse.bass import IndirectOffsetOnAxis
from contextlib import ExitStack

F32 = mybir.dt.float32
BF16 = mybir.dt.bfloat16
I32 = mybir.dt.int32
AF = mybir.ActivationFunctionType
ALU = mybir.AluOpType
AX = mybir.AxisListType
BF = ml_dtypes.bfloat16

S = 2048
E = 1024
H = 16
Dh = 64
NSEQ = 2
TPC = NSEQ * S
NTILE = TPC // 128
NEXP = 32
CAP = 2048
NTC = CAP // 128
NSLOT = NEXP * CAP
WIN = 132
SCW = 134
F2 = 2048
OOB = 1 << 20
NEG3 = -3.0e9
INV_A = 1.0 / 1.702

DEBUG = False
_PL = int(os.environ.get("BASS_PHASE_LIMIT", "99"))

def _pos_bias(d):
    d = np.asarray(d, np.float64)
    return (np.exp(-0.5 * (d / 2.0) ** 2) + np.cos(2 * np.pi * d / 401.0)).astype(np.float32)

def build_consts():
    c = {}
    c["c_ident_bf"] = np.eye(128).astype(BF)
    c["c_ident_f32"] = np.eye(8).astype(np.float32)
    bm = np.full((S, SCW), NEG3, np.float32)
    for t in range(16):
        a = 0 if t == 0 else (128 * t - 2 if t < 15 else S - WIN)
        rows = np.arange(128 * t, 128 * t + 128)
        bm[rows, 0] = _pos_bias(rows)
        for cc in range(2, SCW):
            j = a + cc - 2
            if j < 0 or j >= S or (t == 0 and j == 0):
                continue
            d = rows - j
            ok = np.abs(d) <= 2
            bm[rows[ok], cc] = _pos_bias(d[ok])
    c["c_biasmask"] = bm
    c["c_bias0"] = _pos_bias(np.arange(S).reshape(16, 128).T.copy())
    c["c_iota32"] = np.tile(np.arange(32, dtype=np.float32), (128, 1))
    c["c_iota64"] = c["c_iota32"] + 64.0
    c["c_tokids"] = np.arange(TPC, dtype=np.int32).reshape(NTILE, 128).T.copy()
    c["c_ones_r"] = np.ones((1, 128), BF)
    c["c_ones_c"] = np.ones((128, 1), BF)
    c["c_triu"] = (np.arange(128)[:, None] < np.arange(128)[None, :]).astype(BF)
    c["c_oobfill"] = np.zeros((128, NSLOT // 128), np.int32)
    c["c_zerofill"] = np.zeros((128, NSLOT // 128), np.float32)
    return c

CONST_SHAPES = {
    "c_ident_bf": ([128, 128], BF16), "c_ident_f32": ([8, 8], F32),
    "c_biasmask": ([S, SCW], F32), "c_bias0": ([128, 16], F32),
    "c_iota32": ([128, 32], F32), "c_iota64": ([128, 32], F32),
    "c_tokids": ([128, 32], I32), "c_ones_r": ([1, 128], BF16),
    "c_ones_c": ([128, 1], BF16), "c_triu": ([128, 128], BF16),
    "c_oobfill": ([128, NSLOT // 128], I32), "c_zerofill": ([128, NSLOT // 128], F32),
}

class Sem:
    __slots__ = ("h", "n", "name", "queue")
    def __init__(self, h, name):
        self.h = h; self.n = 0; self.name = name
    def at(self):
        return (self, self.n)

class KB:
    ENGS = ("sync", "scalar", "vector", "tensor", "gpsimd")
    def __init__(self, nc, stack):
        self.nc = nc
        self.stack = stack
        self.q = {e: [] for e in self.ENGS}
        self.bar_sems = {}
        self.dma_sems = []

    def sem(self, name, dma=False, queue=None):
        s = Sem(self.stack.enter_context(self.nc.semaphore(name)), name)
        if dma:
            self.dma_sems.append(s)
            s.queue = queue
        return s

    def emit(self, eng, f=None, waits=(), inc=None, dma=False):
        wl = tuple((s.h, v) for (s, v) in waits if v > 0)
        incl = inc if isinstance(inc, (list, tuple)) else ([inc] if inc is not None else [])
        first = incl[0] if incl else None
        incs = []
        if first is not None:
            k = 16 if dma else 1
            first.n += k
            incs.append((first.h, k))
        def closure(e, wl=wl, f=f, incs=incs):
            for (h, v) in wl:
                e.wait_ge(h, v)
            if f is not None:
                ins = f(e)
                for (h, k) in incs:
                    ins = ins.then_inc(h, k)
        self.q[eng].append(closure)
        for s_ in incl[1:]:
            s_.n += 1
            self.q[eng].append(lambda e, h=s_.h: e.drain().then_inc(h, 1))

    def dr(self, eng):
        self.q[eng].append(lambda e: e.drain())

    def dma(self, eng, out, in_, waits=(), inc=None, transpose=False):
        if transpose:
            self.emit(eng, lambda e, o=out, i=in_: e.dma_start(out=o, in_=i, transpose=True),
                      waits=waits, inc=inc, dma=True)
        else:
            self.emit(eng, lambda e, o=out, i=in_: e.dma_start(out=o, in_=i),
                      waits=waits, inc=inc, dma=True)

    def idma(self, out, out_off, in_, in_off, waits=(), inc=None):
        self.emit("gpsimd", lambda e, o=out, oo=out_off, i=in_, io=in_off:
                  e.indirect_dma_start(o, oo, i, io), waits=waits, inc=inc, dma=True)

    def drain_inc(self, eng, sem, waits=()):
        sems = sem if isinstance(sem, (list, tuple)) else [sem]
        for j, s_ in enumerate(sems):
            self.emit(eng, lambda e: e.drain(), waits=(waits if j == 0 else ()), inc=s_)

    def barrier(self):
        for eng in self.ENGS:
            if eng not in self.bar_sems:
                self.bar_sems[eng] = self.sem("bar_" + eng)
        for eng in self.ENGS:
            own = [(s, s.n) for s in self.dma_sems if s.queue == eng and s.n > 0]
            if own:
                self.emit(eng, None, waits=own)
        for eng in self.ENGS:
            self.drain_inc(eng, self.bar_sems[eng])
        snap = [(self.bar_sems[eng], self.bar_sems[eng].n) for eng in self.ENGS]
        for eng in self.ENGS:
            self.emit(eng, None,
                      waits=[t for t in snap if t[0] is not self.bar_sems[eng]])

    def run(self):
        with self.nc.Block() as block:
            qs = self.q
            @block.sync
            def _(e):
                for c in qs["sync"]: c(e)
            @block.scalar
            def _(e):
                for c in qs["scalar"]: c(e)
            @block.vector
            def _(e):
                for c in qs["vector"]: c(e)
            @block.tensor
            def _(e):
                for c in qs["tensor"]: c(e)
            @block.gpsimd
            def _(e):
                for c in qs["gpsimd"]: c(e)

def build_program(debug=False):
    nc = bass.Bass("TRN2", target_bir_lowering=False)
    stack = ExitStack()
    _pss = []
    _unc = [0]
    def _un(n):
        _unc[0] += 1
        return f"{n}_{_unc[0]}"
    kb = KB(nc, stack)
    sbp = lambda name, shape, dt: stack.enter_context(nc.sbuf_tensor(name, shape, dt))

    inp = {}
    def di(name, shape, dt=F32):
        inp[name] = nc.dram_tensor(name, shape, dt, kind="ExternalInput").ap()
        return inp[name]
    x_in = di("x_shard", [NSEQ, S, E])
    wq = di("L0_attn_wq", [E, E]); bq = di("L0_attn_bq", [E])
    wk = di("L0_attn_wk", [E, E]); bk = di("L0_attn_bk", [E])
    wv = di("L0_attn_wv", [E, E]); bv = di("L0_attn_bv", [E])
    wo = di("L0_attn_wo", [E, E]); bo = di("L0_attn_bo", [E])
    nsc = di("L0_ffn_norm_scale", [E])
    gw = di("L0_ffn_gate_w", [NEXP, E]); gb = di("L0_ffn_gate_b", [NEXP])
    w1 = di("L0_ffn_w1", [NEXP, F2, E]); b1 = di("L0_ffn_b1", [NEXP, F2])
    w2 = di("L0_ffn_w2", [NEXP, E, E]); b2 = di("L0_ffn_b2", [NEXP, E])
    for n, (shape, dt) in CONST_SHAPES.items():
        di(n, shape, dt)

    dbgk = "ExternalOutput" if debug else "Internal"
    def dd(name, shape, dt, dbg=False):
        return nc.dram_tensor(name, shape, dt, kind=(dbgk if dbg else "Internal")).ap()
    wq_bf = dd("wq_bf", [E, E], BF16); wk_bf = dd("wk_bf", [E, E], BF16)
    wv_bf = dd("wv_bf", [E, E], BF16); wo_bf = dd("wo_bf", [E, E], BF16)
    w1_bf = dd("w1_bf", [NEXP, F2, E], BF16)
    w2_bf = dd("w2_bf", [NEXP, E, E], BF16)
    gwhi_d = dd("gwhi_d", [NEXP, E], BF16); gwlo_d = dd("gwlo_d", [NEXP, E], BF16)
    bo_bf = dd("bo_bf", [E], BF16); bv_bf = dd("bv_bf", [E], BF16)
    b1_bf = dd("b1_bf", [NEXP, F2], BF16); b2_bf = dd("b2_bf", [NEXP, E], BF16)
    lnx_d = dd("lnx_d", [S, E], BF16, dbg=True)
    v_d = dd("v_d", [NSEQ, S + 1, E], BF16, dbg=True)
    x1_d = dd("x1_d", [TPC, E], F32, dbg=True)
    tf_d = dd("tf_d", [TPC, E], BF16, dbg=True)
    tflo_d = dd("tflo_d", [TPC, E], BF16)
    toki_d = dd("toki_d", [NSLOT, 1], I32, dbg=True)
    comb_d = dd("comb_d", [NSLOT, 1], F32, dbg=True)
    slots_d = dd("slots_d", [NTILE, 4, 128], I32, dbg=True)
    packed_d = dd("packed_d", [NSLOT, E], BF16, dbg=True)
    outp_d = dd("outp_d", [NSLOT, E], F32, dbg=True)
    row0_d = dd("row0_d", [NSEQ * H, Dh], F32)
    out_d = nc.dram_tensor("out", [NSEQ, S, E], F32, kind="ExternalOutput").ap()

    dS = kb.sem("dS", dma=True, queue="sync")
    dA = kb.sem("dA", dma=True, queue="scalar")
    dP = kb.sem("dP", dma=True, queue="gpsimd")
    sC = kb.sem("sC", dma=True, queue="sync")
    sLX = kb.sem("sLX", dma=True, queue="sync")
    sX = [kb.sem("sX0", dma=True, queue="sync"), kb.sem("sX1", dma=True, queue="sync")]
    sWQ = kb.sem("sWQ", dma=True, queue="sync"); sWK = kb.sem("sWK", dma=True, queue="sync")
    sWV = kb.sem("sWV", dma=True, queue="sync"); sWO = kb.sem("sWO", dma=True, queue="sync")
    sVW = [kb.sem("sVW0", dma=True, queue="sync"), kb.sem("sVW1", dma=True, queue="sync")]
    sTFT = [kb.sem("sTFT0", dma=True, queue="sync"), kb.sem("sTFT1", dma=True, queue="sync")]
    sWE = [kb.sem("sWE0", dma=True, queue="sync"), kb.sem("sWE1", dma=True, queue="sync")]
    sLS = [kb.sem("sLS0", dma=True, queue="scalar"), kb.sem("sLS1", dma=True, queue="scalar")]
    sVD = kb.sem("sVD", dma=True, queue="scalar")
    sTS = [kb.sem("sTS0", dma=True, queue="scalar"), kb.sem("sTS1", dma=True, queue="scalar")]
    sG = kb.sem("sG", dma=True, queue="scalar")
    sPRE = kb.sem("sPRE", dma=True, queue="scalar")
    sSD = kb.sem("sSD", dma=True, queue="scalar")
    sOP = [kb.sem("sOP0", dma=True, queue="scalar"), kb.sem("sOP1", dma=True, queue="scalar")]
    sOUT = kb.sem("sOUT", dma=True, queue="scalar")
    sGW = kb.sem("sGW", dma=True, queue="sync")
    sCATT = kb.sem("sCATT", dma=True, queue="gpsimd")
    sCAST = kb.sem("sCAST", dma=True, queue="gpsimd")
    sROW = kb.sem("sROW", dma=True, queue="gpsimd")
    sCM = [kb.sem("sCM0", dma=True, queue="gpsimd"), kb.sem("sCM1", dma=True, queue="gpsimd")]
    sFG = [kb.sem("sFG0", dma=True, queue="gpsimd"), kb.sem("sFG1", dma=True, queue="gpsimd")]
    sPK = [kb.sem("sPK0", dma=True, queue="gpsimd"), kb.sem("sPK1", dma=True, queue="gpsimd")]
    tS = kb.sem("tS"); vS = kb.sem("vS"); aS = kb.sem("aS"); pS = kb.sem("pS")
    vF = kb.sem("vF"); aF = kb.sem("aF"); tF = kb.sem("tF")
    vB = kb.sem("vB"); aB = kb.sem("aB")

    ident_bf = sbp("ident_bf", [128, 128], BF16)
    ident_f32 = sbp("ident_f32", [8, 8], F32)
    biasmask = sbp("biasmask", [128, 16 * SCW], F32)
    bias0 = sbp("bias0", [128, 16], F32)
    iota32 = sbp("iota32", [128, 32], F32)
    iota64 = sbp("iota64", [128, 32], F32)
    tokids = sbp("tokids", [128, 32], I32)
    ones_r = sbp("ones_r", [1, 128], BF16)
    ones_c = sbp("ones_c", [128, 1], BF16)
    triu = sbp("triu", [128, 128], BF16)
    bq_pp = sbp("bq_pp", [128, 8], F32)
    bk_pp = sbp("bk_pp", [128, 8], F32)
    scale_bc = sbp("scale_bc", [128, E], F32)
    bo_row = sbp("bo_row", [1, E], BF16)
    bv_row = sbp("bv_row", [1, E], BF16)
    gbhi_row = sbp("gbhi_row", [1, NEXP], BF16)
    gblo_row = sbp("gblo_row", [1, NEXP], BF16)
    gwT_hi = sbp("gwT_hi", [128, 8 * NEXP], BF16)
    gwT_lo = sbp("gwT_lo", [128, 8 * NEXP], BF16)
    PA = stack.enter_context(nc.psum_tensor("PA", [128, 2048], F32)).ap()
    _PBh = stack.enter_context(nc.psum_tensor("PB", [128, 2048], F32))
    PB16 = _PBh.bitcast(BF16).ap()
    PB = _PBh.ap()

    def _finish():
        kb.barrier()
        kb.emit("scalar", None, waits=[sOUT.at()])
        kb.run()
        for _p in _pss:
            _p.close()
        stack.close()
    for (dst, src) in [(ident_bf, inp["c_ident_bf"]), (ident_f32, inp["c_ident_f32"]),
                       (bias0, inp["c_bias0"]), (iota32, inp["c_iota32"]),
                       (iota64, inp["c_iota64"]), (tokids, inp["c_tokids"]),
                       (ones_r, inp["c_ones_r"]), (ones_c, inp["c_ones_c"]),
                       (triu, inp["c_triu"])]:
        kb.dma("sync", dst[:], src[:, :], inc=sC)
    kb.dma("sync", biasmask[:].rearrange("p (t w) -> p t w", w=SCW),
           inp["c_biasmask"].rearrange("(t p) w -> p t w", p=128), inc=sC)
    fillo = sbp("fillo", [128, NSLOT // 128], I32)
    fillz = sbp("fillz", [128, NSLOT // 128], F32)
    kb.dma("sync", fillo[:], inp["c_oobfill"][:, :], inc=sC)
    kb.dma("sync", fillz[:], inp["c_zerofill"][:, :], inc=sC)

    kb.dma("gpsimd", wq_bf[:, :], wq[:, :], inc=sCATT)
    kb.dma("gpsimd", wk_bf[:, :], wk[:, :], inc=sCATT)
    kb.dma("gpsimd", wv_bf[:, :], wv[:, :], inc=sCATT)
    kb.dma("gpsimd", wo_bf[:, :], wo[:, :], inc=sCATT)
    kb.dma("gpsimd", bo_bf[:], bo[:], inc=sCATT)
    kb.dma("gpsimd", bv_bf[:], bv[:], inc=sCATT)
    attn_cast_done = sCATT.at()
    cast_done = {}
    cast_queue = list(range(NEXP))
    def emit_casts(n):
        for _ in range(n):
            if not cast_queue:
                return
            e_ = cast_queue.pop(0)
            kb.dma("gpsimd", w1_bf[e_, :, :], w1[e_, :, :], inc=sCAST)
            kb.dma("gpsimd", w2_bf[e_, :, :], w2[e_, :, :], inc=sCAST)
            kb.dma("gpsimd", b1_bf[e_, :], b1[e_, :], inc=sCAST)
            kb.dma("gpsimd", b2_bf[e_, :], b2[e_, :], inc=sCAST)
            cast_done[e_] = sCAST.at()

    _ps0 = _ps = ExitStack(); _pss.append(_ps)
    p0a = PA[:, 0:8]
    p0b = PA[:, 512:520]
    p0c = PB[:, 0:E]
    t8 = _ps.enter_context(nc.sbuf_tensor(_un("t8"), [8, 128], F32))
    t8b = _ps.enter_context(nc.sbuf_tensor(_un("t8b"), [8, 128], F32))
    nrow = _ps.enter_context(nc.sbuf_tensor(_un("nrow"), [1, E], F32))
    nhi = _ps.enter_context(nc.sbuf_tensor(_un("nhi"), [1, E], BF16))
    nlo_f = _ps.enter_context(nc.sbuf_tensor(_un("nlo_f"), [1, E], F32))
    nlo = _ps.enter_context(nc.sbuf_tensor(_un("nlo"), [1, E], BF16))
    grow = _ps.enter_context(nc.sbuf_tensor(_un("grow"), [NEXP, E], F32))
    ghi = _ps.enter_context(nc.sbuf_tensor(_un("ghi"), [NEXP, E], BF16))
    glo = _ps.enter_context(nc.sbuf_tensor(_un("glo"), [NEXP, E], BF16))
    gbrow = _ps.enter_context(nc.sbuf_tensor(_un("gbrow"), [1, NEXP], F32))
    gblo_f = _ps.enter_context(nc.sbuf_tensor(_un("gblo_f"), [1, NEXP], F32))
    if True:
        kb.dma("sync", t8[:], bq.rearrange("(a b) -> a b", b=128), inc=sC)
        kb.dma("sync", t8b[:], bk.rearrange("(a b) -> a b", b=128), inc=sC)
        kb.dma("sync", nrow[:], nsc[None, :], inc=sC)
        kb.dma("sync", grow[:], gw[:, :], inc=sC)
        kb.dma("sync", gbrow[:], gb[None, :], inc=sC)
        w0 = sC.at()
        w1n = w0
        kb.emit("tensor", lambda e: e.transpose(p0a, t8[:], ident_f32[:]), waits=[w0], inc=tS)
        kb.emit("tensor", lambda e: e.transpose(p0b, t8b[:], ident_f32[:]), inc=tS)
        w_tr = tS.at()
        kb.emit("vector", lambda e: e.tensor_scalar(bq_pp[:], p0a, 0.125, None, ALU.mult),
                waits=[w_tr], inc=vS)
        kb.emit("vector", lambda e: e.tensor_copy(bk_pp[:], p0b), inc=vS)
        kb.emit("vector", lambda e: e.tensor_copy(nhi[:], nrow[:]), waits=[w1n], inc=vS)
        kb.emit("vector", lambda e: e.tensor_sub(nlo_f[:], nrow[:], nhi[:]), inc=vS)
        kb.emit("vector", lambda e: e.tensor_copy(nlo[:], nlo_f[:]), inc=vS)
        kb.emit("vector", lambda e: e.tensor_copy(ghi[:], grow[:]), inc=vS)
        kb.emit("vector", lambda e: e.tensor_sub(glo[:], grow[:], ghi[:]), inc=vS)
        kb.emit("vector", lambda e: e.tensor_copy(gbhi_row[:], gbrow[:]), inc=vS)
        kb.dr("vector")
        kb.emit("vector", lambda e: e.tensor_sub(gblo_f[:], gbrow[:], gbhi_row[:]), inc=vS)
        kb.dr("vector")
        kb.emit("vector", lambda e: e.tensor_copy(gblo_row[:], gblo_f[:]), inc=vS)
        kb.drain_inc("vector", vF)
        w_vf = vF.at()
        kb.emit("tensor", lambda e: e.matmul(p0c[:, 0:512], ones_r[:], nhi[:, 0:512],
                                             start=True, stop=False), waits=[w_vf])
        kb.emit("tensor", lambda e: e.matmul(p0c[:, 512:E], ones_r[:], nhi[:, 512:E],
                                             start=True, stop=False))
        kb.emit("tensor", lambda e: e.matmul(p0c[:, 0:512], ones_r[:], nlo[:, 0:512],
                                             start=False, stop=True))
        kb.emit("tensor", lambda e: e.matmul(p0c[:, 512:E], ones_r[:], nlo[:, 512:E],
                                             start=False, stop=True), inc=tS)
        kb.emit("scalar", lambda e: e.copy(scale_bc[:], p0c), waits=[tS.at()], inc=aS)
        kb.dma("scalar", gwhi_d[:, :], ghi[:], waits=[w_vf], inc=sG)
        kb.dma("scalar", gwlo_d[:, :], glo[:], inc=sG)
        wg = sG.at()
        for hc in range(8):
            kb.dma("sync", gwT_hi[:, hc * NEXP:(hc + 1) * NEXP],
                   gwhi_d[:, hc * 128:(hc + 1) * 128],
                   waits=([wg] if hc == 0 else ()), inc=sGW, transpose=True)
            kb.dma("sync", gwT_lo[:, hc * NEXP:(hc + 1) * NEXP],
                   gwlo_d[:, hc * 128:(hc + 1) * 128], inc=sGW, transpose=True)
        kb.dma("sync", bo_row[:], bo_bf[None, :], waits=[attn_cast_done], inc=sGW)
        kb.dma("sync", bv_row[:], bv_bf[None, :], inc=sGW)
        kb.dma("scalar", toki_d.rearrange("(p c) one -> p (c one)", p=128), fillo[:],
               waits=[w0], inc=sPRE)
        kb.dma("scalar", comb_d.rearrange("(p c) one -> p (c one)", p=128), fillz[:], inc=sPRE)
        prefill_done = sPRE.at()
        kb.barrier()

    _ps0.close()
    if _PL <= 1:
        _finish(); return nc
    ARENA = sbp("ARENA", [128, 65536], BF16)
    QT = ARENA[:, 0:16384]
    KT = ARENA[:, 16384:32768]
    AFM = ARENA[:, 32768:49152]
    LNXT = AFM
    wTs = ARENA[:, 49152:65536]

    def wT_chunk(par, hc):
        base = par * 8 * E + hc * E
        return wTs[:, base:base + E]

    def load_wT(par, wsrc, sem, waits=()):
        for hc in range(8):
            kb.dma("sync", wT_chunk(par, hc), wsrc[:, 128 * hc:128 * (hc + 1)],
                   waits=(waits if hc == 0 else ()), inc=sem, transpose=True)
        return sem.at()

    def ln_stats(tile_ap, mean_t, rstd_t, scr, st4, wait_src, psum_src=False):
        kb.emit("vector", lambda e: e.tensor_reduce(st4[:, 0:1], tile_ap, AX.X, ALU.add),
                waits=wait_src, inc=([vS, vF] if psum_src else vS))
        kb.emit("scalar", lambda e: e.activation(scr, tile_ap, AF.Square, accum_out=st4[:, 1:2]),
                waits=(list(wait_src) + [vF.at()] if psum_src else wait_src))
        kb.drain_inc("scalar", aF)
        a_sq = aF.at()
        kb.dr("vector")
        kb.emit("vector", lambda e: e.tensor_scalar(mean_t, st4[:, 0:1], 1.0 / E, None, ALU.mult), inc=vS)
        kb.emit("vector", lambda e: e.tensor_scalar(st4[:, 3:4], st4[:, 1:2], 1.0 / E, 1e-5,
                                                    ALU.mult, ALU.add), waits=[a_sq], inc=vS)
        kb.dr("vector")
        kb.emit("vector", lambda e: e.tensor_mul(st4[:, 2:3], mean_t, mean_t), inc=vS)
        kb.dr("vector")
        kb.emit("vector", lambda e: e.tensor_sub(st4[:, 3:4], st4[:, 3:4], st4[:, 2:3]), inc=vS)
        kb.drain_inc("vector", vF)
        v_var = vF.at()
        kb.emit("scalar", lambda e: e.activation(st4[:, 3:4], st4[:, 3:4], AF.Ln), waits=[v_var])
        kb.dr("scalar")
        kb.emit("scalar", lambda e: e.activation(rstd_t, st4[:, 3:4], AF.Exp, scale=-0.5))
        kb.drain_inc("scalar", aF)
        return aF.at()

    _psSEQ = ExitStack(); _pss.append(_psSEQ)
    _sq = lambda name, shape, dt: _psSEQ.enter_context(nc.sbuf_tensor(name, shape, dt))
    lnbf = _sq("lnbf", [128, 2 * E], BF16)
    vsb = _sq("vsb", [128, 2 * E], BF16)
    mrow = _sq("mrow", [1, E], BF16)
    r0sc = _sq("r0sc", [16, E], F32)
    r0d = _sq("r0d", [16, 1], F32)
    x1t = _sq("x1t", [128, 2 * E], F32)
    tfh = _sq("tfh", [128, 2 * E], BF16)
    tfl = _sq("tfl", [128, 2 * E], BF16)
    q0blk = _sq("q0blk", [128, 8 * 16], BF16)
    etile = _sq("etile", [128, 16 * 16], BF16)
    xbuf = _sq("xbuf", [128, 2 * E], F32)
    lst = _sq("lst", [128, 16], F32)
    e0s1 = _sq("e0s1", [128, 2 * 16], F32)
    e0ge = _sq("e0ge", [128, 2 * 16], F32)
    e0ex = _sq("e0ex", [128, 2 * 16], F32)
    vwa = _sq("vwa", [128, 2 * E], BF16)
    vwb = _sq("vwb", [6, 2 * E], BF16)
    sm1 = _sq("sm1", [128, 2 * SCW], F32)
    smge = _sq("smge", [128, 2 * SCW], F32)
    sme = _sq("sme", [128, 2 * SCW], F32)
    smn = _sq("smn", [128, 16], F32)
    probs = _sq("probs", [128, 2 * SCW], BF16)
    ptA = _sq("ptA", [128, 2 * 128], BF16)
    ptB = _sq("ptB", [6, 2 * 128], BF16)
    l3t = _sq("l3t", [128, E], F32)
    pst = _sq("pst", [128, 32], F32)
    AR32 = ARENA.bitcast(F32)
    tff = AR32[:, 6 * E:7 * E]
    for s in range(NSEQ):
        _psA = _ps = ExitStack(); _pss.append(_ps)
        if True:
            lnbf_free = [0, 0]
            for i in range(16):
                par = i % 2
                xt = xbuf[:, par * E:(par + 1) * E]
                sc = AR32[:, par * E:(par + 1) * E]
                st = lst[:, par * 8:par * 8 + 8]
                lb = lnbf[:, par * E:(par + 1) * E]
                kb.dma("sync", xt, x_in[s, 128 * i:128 * (i + 1), :],
                       waits=([(sLS[par], lnbf_free[par])] if i >= 2 else ()), inc=sX[par])
                w_x = sX[par].at()
                a_rstd = ln_stats(xt, st[:, 0:1], st[:, 1:2], sc, st[:, 2:6], [w_x])
                kb.emit("vector", lambda e, xt=xt, st=st, lb=lb:
                        e.tensor_scalar(lb, xt, st[:, 0:1], st[:, 1:2], ALU.subtract, ALU.mult),
                        waits=[a_rstd], inc=vS)
                kb.drain_inc("vector", vF)
                kb.dma("scalar", lnx_d[128 * i:128 * (i + 1), :], lb, waits=[vF.at()], inc=sLS[par])
                lnbf_free[par] = sLS[par].n
            kb.emit("sync", None, waits=[sLS[0].at(), sLS[1].at()])
            for hc in range(8):
                kb.dma("sync", LNXT[:, hc * S:(hc + 1) * S],
                       lnx_d[:, 128 * hc:128 * (hc + 1)], inc=sLX, transpose=True)
            lnxT_done = sLX.at()
        _psA.close()
        emit_casts(3)
        if _PL == 11: break

        wq_done = load_wT(0, wq_bf, sWQ, waits=[attn_cast_done])
        wk_done = load_wT(1, wk_bf, sWK)
        _psB = _ps = ExitStack(); _pss.append(_ps)
        pQ = PA
        if True:
            first_qc = True
            for wi, (dst, w_done, bias_pp, scale) in enumerate(
                    [(QT, wq_done, bq_pp, 0.125), (KT, wk_done, bk_pp, 1.0)]):
                for qc in range(8):
                    for hc in range(8):
                        lhsT = wT_chunk(wi, hc)[:, 128 * qc:128 * (qc + 1)]
                        for tc in range(4):
                            waits = []
                            if hc == 0 and tc == 0:
                                waits = [w_done, lnxT_done]
                                if not first_qc:
                                    waits.append(aB.at())
                            kb.emit("tensor", lambda e, l=lhsT, tc=tc, hc=hc:
                                    e.matmul(pQ[:, 512 * tc:512 * (tc + 1)], l,
                                             LNXT[:, hc * S + 512 * tc: hc * S + 512 * (tc + 1)],
                                             start=(hc == 0), stop=(hc == 7)),
                                    waits=waits, inc=(tF if (hc == 7 and tc == 3) else None))
                    first_qc = False
                    w_mm = tF.at()
                    for tc in range(4):
                        kb.emit("scalar", lambda e, dst=dst, qc=qc, tc=tc, b=bias_pp, sc=scale:
                                e.activation(dst[:, qc * S + 512 * tc: qc * S + 512 * (tc + 1)],
                                             pQ[:, 512 * tc:512 * (tc + 1)], AF.Identity,
                                             bias=b[:, qc:qc + 1], scale=sc),
                                waits=([w_mm] if tc == 0 else ()))
                    kb.drain_inc("scalar", aB)
            kb.drain_inc("scalar", aF)
        _psB.close()
        qk_done = aF.at()
        if _PL == 12: break

        wv_done = load_wT(0, wv_bf, sWV, waits=[qk_done])
        _psC = _ps = ExitStack(); _pss.append(_ps)
        if True:
            pSc0 = [PB[:, 0:16], PB[:, 512:528]]
            kb.emit("vector", lambda e: e.memset(q0blk[:], 0.0), waits=[qk_done], inc=vS)
            kb.drain_inc("vector", vF)
            w_z = vF.at()
            for h in range(H):
                dc, ro = h // 2, (h % 2) * 64
                kb.emit("scalar", lambda e, dc=dc, ro=ro, h=h:
                        e.copy(q0blk[ro:ro + 64, dc * 16 + h:dc * 16 + h + 1],
                               QT[ro:ro + 64, dc * S:dc * S + 1]),
                        waits=([w_z, qk_done] if h == 0 else ()))
            kb.drain_inc("scalar", aF)
            q0_done = aF.at()
            for jc in range(16):
                par = jc % 2
                cs = slice(par * 16, par * 16 + 16)
                for dc in range(8):
                    waits = []
                    if dc == 0:
                        waits = [q0_done] if jc < 2 else [(vB, vB.n)]
                    kb.emit("tensor", lambda e, dc=dc, jc=jc, par=par:
                            e.matmul(pSc0[par],
                                     KT[:, dc * S + 128 * jc:dc * S + 128 * (jc + 1)],
                                     q0blk[:, dc * 16:(dc + 1) * 16],
                                     start=(dc == 0), stop=(dc == 7)),
                            waits=waits, inc=(tF if dc == 7 else None))
                w_sc = tF.at()
                kb.emit("vector", lambda e, cs=cs, jc=jc, par=par:
                        e.tensor_scalar(e0s1[:, cs], pSc0[par], bias0[:, jc:jc + 1],
                                        None, ALU.add), waits=[w_sc], inc=[vS, vB])
                kb.dr("vector")
                kb.emit("vector", lambda e, cs=cs:
                        e.tensor_scalar(e0ge[:, cs], e0s1[:, cs], 0.0, None, ALU.is_ge), inc=vS)
                kb.drain_inc("vector", vF)
                kb.emit("scalar", lambda e, cs=cs:
                        e.activation(e0ex[:, cs], e0s1[:, cs], AF.Exp), waits=[vF.at()])
                kb.drain_inc("scalar", aF)
                kb.emit("vector", lambda e, cs=cs, jc=jc:
                        e.tensor_mul(etile[:, 16 * jc:16 * (jc + 1)], e0ex[:, cs], e0ge[:, cs]),
                        waits=[aF.at()], inc=vS)
            kb.drain_inc("vector", vF)
            e_done = vF.at()

        _psC.close()
        if _PL == 13: break
        wo_done = load_wT(1, wo_bf, sWO, waits=[qk_done])
        _psD = _ps = ExitStack(); _pss.append(_ps)
        pV = PA[:, 0:E]
        pMean = PB[0:1, 0:E]
        pNum = PA[0:16, E:2 * E]
        pDen = PB[0:16, E:E + 1]
        if True:
            for i in range(16):
                par = i % 2
                vt = vsb[:, par * E:(par + 1) * E]
                for hc in range(8):
                    waits = []
                    if hc == 0:
                        waits = ([wv_done, e_done] if i == 0 else [(aB, aB.n)])
                    kb.emit("tensor", lambda e, hc=hc, i=i:
                            e.matmul(pV[:, 0:512],
                                     LNXT[:, hc * S + 128 * i:hc * S + 128 * (i + 1)],
                                     wT_chunk(0, hc)[:, 0:512],
                                     start=(hc == 0), stop=False), waits=waits)
                    kb.emit("tensor", lambda e, hc=hc, i=i:
                            e.matmul(pV[:, 512:E],
                                     LNXT[:, hc * S + 128 * i:hc * S + 128 * (i + 1)],
                                     wT_chunk(0, hc)[:, 512:E],
                                     start=(hc == 0), stop=False))
                kb.emit("tensor", lambda e: e.matmul(pV[:, 0:512], ones_r[:],
                                                     bv_row[:, 0:512], start=False, stop=True))
                kb.emit("tensor", lambda e: e.matmul(pV[:, 512:E], ones_r[:],
                                                     bv_row[:, 512:E], start=False, stop=True),
                        inc=tF)
                w_vp = tF.at()
                kb.emit("scalar", lambda e, vt=vt: e.copy(vt, pV),
                        waits=[w_vp, (sVD, sVD.n), (tF, tF.n)])
                kb.drain_inc("scalar", aF)
                w_ev = aF.at()
                kb.emit("tensor", lambda e, vt=vt, i=i:
                        e.matmul(pMean[:, 0:512], ones_c[:], vt[:, 0:512],
                                 start=(i == 0), stop=(i == 15), skip_group_check=True),
                        waits=[w_ev])
                kb.emit("tensor", lambda e, vt=vt, i=i:
                        e.matmul(pMean[:, 512:E], ones_c[:], vt[:, 512:E],
                                 start=(i == 0), stop=(i == 15), skip_group_check=True))
                kb.emit("tensor", lambda e, vt=vt, i=i:
                        e.matmul(pNum[:, 0:512], etile[:, 16 * i:16 * (i + 1)], vt[:, 0:512],
                                 start=(i == 0), stop=(i == 15), skip_group_check=True))
                kb.emit("tensor", lambda e, vt=vt, i=i:
                        e.matmul(pNum[:, 512:E], etile[:, 16 * i:16 * (i + 1)], vt[:, 512:E],
                                 start=(i == 0), stop=(i == 15), skip_group_check=True))
                kb.emit("tensor", lambda e, i=i:
                        e.matmul(pDen, etile[:, 16 * i:16 * (i + 1)], ones_c[:],
                                 start=(i == 0), stop=(i == 15), skip_group_check=True), inc=tF)
                acc_done = tF.at()
                kb.dma("scalar", v_d[s, 128 * i:128 * (i + 1), :], vt,
                       waits=[acc_done], inc=sVD)
                kb.drain_inc("scalar", aB)
            w_acc = tF.at()
            kb.emit("scalar", lambda e: e.activation(mrow[:], pMean, AF.Copy,
                                                     scale=1.0 / S), waits=[w_acc])
            kb.drain_inc("scalar", aF)
            kb.dma("scalar", v_d[s, S:S + 1, :], mrow[:], waits=[aF.at()], inc=sVD)
            kb.emit("vector", lambda e: e.reciprocal(r0d[:], pDen), waits=[w_acc], inc=vS)
            kb.drain_inc("vector", vF)
            kb.emit("vector", lambda e: e.tensor_scalar(r0sc[:], pNum, r0d[:, 0:1],
                                                        None, ALU.mult),
                    waits=[vF.at()], inc=vS)
            kb.drain_inc("vector", vF)
            w_r0 = vF.at()
            for h in range(H):
                kb.dma("gpsimd", row0_d[s * H + h:s * H + h + 1, :],
                       r0sc[h:h + 1, Dh * h:Dh * (h + 1)],
                       waits=([w_r0] if h == 0 else ()), inc=sROW)
            row0_stored = sROW.at()
            v_phase_done = sVD.at()
        _psD.close()
        kb.barrier()
        if _PL == 14: break

        _psE = _ps = ExitStack(); _pss.append(_ps)
        if True:
            pSc = [PA[:, 0:SCW], PA[:, 512:512 + SCW]]
            pPTa = [PB16[:, 0:128], PB16[:, 1024:1152]]
            pPTb = [PB16[0:6, 2048:2176], PB16[0:6, 3072:3200]]
            pCtx = [PA[0:64, 1024:1152], PA[0:64, 1536:1664]]
            it = 0
            vw_used = {}
            for rt in range(16):
                a = 0 if rt == 0 else (128 * rt - 2 if rt < 15 else S - WIN)
                wpar = rt % 2
                wv_waits = [v_phase_done]
                if rt >= 2:
                    wv_waits.append((tF, vw_used[rt - 2]))
                kb.dma("sync", vwa[0:1, wpar * E:(wpar + 1) * E], v_d[s, 0:1, :],
                       waits=wv_waits, inc=sVW[wpar])
                kb.dma("sync", vwa[1:2, wpar * E:(wpar + 1) * E], v_d[s, S:S + 1, :], inc=sVW[wpar])
                kb.dma("sync", vwa[2:128, wpar * E:(wpar + 1) * E],
                       v_d[s, a:a + 126, :], inc=sVW[wpar])
                kb.dma("sync", vwb[0:6, wpar * E:(wpar + 1) * E],
                       v_d[s, a + 126:a + 132, :], inc=sVW[wpar])
                w_vw = sVW[wpar].at()
                for h in range(H):
                    par = it % 2
                    dc, ro = h // 2, (h % 2) * 64
                    scs = slice(par * SCW, par * SCW + SCW)
                    qsl = QT[ro:ro + 64, dc * S + 128 * rt: dc * S + 128 * (rt + 1)]
                    sc_waits = [qk_done] if it < 2 else [(vB, vB.n)]
                    kb.emit("tensor", lambda e, q=qsl, dc=dc, ro=ro, par=par:
                            e.matmul(pSc[par][:, 0:2],
                                     q, KT[ro:ro + 64, dc * S:dc * S + 2],
                                     start=True, stop=True), waits=sc_waits)
                    kb.emit("tensor", lambda e, q=qsl, dc=dc, ro=ro, par=par, a=a:
                            e.matmul(pSc[par][:, 2:SCW],
                                     q, KT[ro:ro + 64, dc * S + a:dc * S + a + WIN],
                                     start=True, stop=True), inc=tF)
                    w_sc = tF.at()
                    n0 = smn[:, par * 8 + 0:par * 8 + 1]
                    n1 = smn[:, par * 8 + 1:par * 8 + 2]
                    n2 = smn[:, par * 8 + 2:par * 8 + 3]
                    n3 = smn[:, par * 8 + 3:par * 8 + 4]
                    n4 = smn[:, par * 8 + 4:par * 8 + 5]
                    n5 = smn[:, par * 8 + 5:par * 8 + 6]
                    n7 = smn[:, par * 8 + 7:par * 8 + 8]
                    sm_war = [(aF, aF.n)] if it >= 2 else []
                    kb.emit("vector", lambda e, scs=scs, rt=rt, par=par:
                            e.tensor_tensor(sm1[:, scs], pSc[par],
                                            biasmask[:, rt * SCW:(rt + 1) * SCW], ALU.add),
                            waits=[w_sc] + sm_war, inc=[vS, vB])
                    kb.emit("vector", lambda e, scs=scs:
                            e.tensor_scalar(smge[:, scs], sm1[:, scs], 0.0, None, ALU.is_ge),
                            inc=vS)
                    kb.emit("vector", lambda e, scs=scs, n0=n0:
                            e.tensor_reduce(n0, sm1[:, scs], AX.X, ALU.max, negate=True), inc=vS)
                    kb.dr("vector")
                    kb.emit("vector", lambda e, n0=n0, n1=n1:
                            e.tensor_scalar_min(n1, n0, 0.0), inc=vS)
                    kb.drain_inc("vector", vF)
                    kb.emit("scalar", lambda e, scs=scs, n1=n1:
                            e.activation(sme[:, scs], sm1[:, scs], AF.Exp, bias=n1),
                            waits=[vF.at()])
                    kb.drain_inc("scalar", aF)
                    kb.emit("vector", lambda e, scs=scs:
                            e.tensor_mul(sme[:, scs], sme[:, scs], smge[:, scs]),
                            waits=[aF.at()], inc=vS)
                    kb.dr("vector")
                    kb.emit("vector", lambda e, scs=scs, n2=n2:
                            e.tensor_reduce(n2, sme[:, scs], AX.X, ALU.add), inc=vS)
                    kb.emit("vector", lambda e, n0=n0, n3=n3:
                            e.tensor_scalar(n3, n0, 0.0, None, ALU.is_gt), inc=vS)
                    kb.dr("vector")
                    kb.emit("vector", lambda e, n2=n2, n3=n3, n4=n4:
                            e.tensor_add(n4, n2, n3), inc=vS)
                    kb.drain_inc("vector", vF)
                    kb.emit("vector", lambda e, n4=n4, n5=n5: e.reciprocal(n5, n4),
                            waits=[vF.at()], inc=vS)
                    kb.drain_inc("vector", vF)
                    kb.emit("vector", lambda e, scs=scs, n5=n5:
                            e.tensor_scalar(probs[:, scs], sme[:, scs], n5, None, ALU.mult),
                            waits=[vF.at(), (tF, tF.n)], inc=vS)
                    kb.emit("vector", lambda e, scs=scs, n3=n3:
                            e.tensor_copy(probs[:, scs.start + 1:scs.start + 2], n3), inc=vS)
                    kb.drain_inc("vector", vF)
                    w_pr = vF.at()
                    pt_war = [(aB, aB.n)] if it >= 2 else []
                    kb.emit("tensor", lambda e, par=par, scs=scs:
                            e.transpose(pPTa[par], probs[:, scs.start:scs.start + 128],
                                        ident_bf[:]),
                            waits=[w_pr] + pt_war)
                    kb.emit("tensor", lambda e, par=par, scs=scs:
                            e.transpose(pPTb[par], probs[:, scs.start + 128:scs.start + SCW],
                                        ident_bf[:]), inc=tF)
                    w_pt = tF.at()
                    kb.emit("scalar", lambda e, par=par:
                            e.copy(ptA[:, par * 128:(par + 1) * 128], pPTa[par]),
                            waits=[w_pt])
                    kb.emit("scalar", lambda e, par=par:
                            e.copy(ptB[:, par * 128:(par + 1) * 128], pPTb[par]))
                    kb.drain_inc("scalar", [aF, aB])
                    w_pte = aF.at()
                    kb.emit("tensor", lambda e, par=par, h=h, wpar=wpar:
                            e.matmul(pCtx[par],
                                     vwa[:, wpar * E + Dh * h: wpar * E + Dh * (h + 1)],
                                     ptA[:, par * 128:(par + 1) * 128],
                                     start=True, stop=False),
                            waits=[w_pte, w_vw])
                    kb.emit("tensor", lambda e, par=par, h=h, wpar=wpar:
                            e.matmul(pCtx[par],
                                     vwb[:, wpar * E + Dh * h: wpar * E + Dh * (h + 1)],
                                     ptB[:, par * 128:(par + 1) * 128],
                                     start=False, stop=True), inc=tF)
                    w_ctx = tF.at()
                    vw_used[rt] = tF.n
                    kb.emit("scalar", lambda e, par=par, dc=dc, ro=ro, rt=rt:
                            e.copy(AFM[ro:ro + 64, dc * S + 128 * rt: dc * S + 128 * (rt + 1)],
                                   pCtx[par]), waits=[w_ctx])
                    kb.drain_inc("scalar", [aS, aB])
                    it += 1
            scores_done = aS.at()
        _psE.close()
        for h in range(H):
            dc, ro = h // 2, (h % 2) * 64
            kb.dma("gpsimd", AFM[ro:ro + 64, dc * S:dc * S + 1],
                   row0_d[s * H + h:s * H + h + 1, :],
                   waits=([scores_done, row0_stored] if h == 0 else ()), inc=sROW)
        row0_fixed = sROW.at()
        emit_casts(3)
        if _PL == 15: break

        _psF = _ps = ExitStack(); _pss.append(_ps)
        pP = PA
        if True:
            tail_done = [0, 0]
            for tc in range(16):
                par = tc % 2
                ppar = slice(par * E, (par + 1) * E)
                st = pst[:, par * 16:par * 16 + 16]
                for hc in range(8):
                    waits = []
                    if hc == 0:
                        waits = ([wo_done, row0_fixed] if tc < 2 else [(vB, vB.n)])
                    kb.emit("tensor", lambda e, hc=hc, tc=tc, ppar=ppar:
                            e.matmul(pP[:, ppar.start:ppar.start + 512],
                                     AFM[:, hc * S + 128 * tc: hc * S + 128 * (tc + 1)],
                                     wT_chunk(1, hc)[:, 0:512],
                                     start=(hc == 0), stop=False), waits=waits)
                    kb.emit("tensor", lambda e, hc=hc, tc=tc, ppar=ppar:
                            e.matmul(pP[:, ppar.start + 512:ppar.start + E],
                                     AFM[:, hc * S + 128 * tc: hc * S + 128 * (tc + 1)],
                                     wT_chunk(1, hc)[:, 512:E],
                                     start=(hc == 0), stop=False))
                kb.emit("tensor", lambda e, ppar=ppar:
                        e.matmul(pP[:, ppar.start:ppar.start + 512], ones_r[:],
                                 bo_row[:, 0:512], start=False, stop=True))
                kb.emit("tensor", lambda e, ppar=ppar:
                        e.matmul(pP[:, ppar.start + 512:ppar.start + E], ones_r[:],
                                 bo_row[:, 512:E], start=False, stop=True), inc=tF)
                w_pj = tF.at()
                kb.dma("sync", xbuf[:, ppar], x_in[s, 128 * tc:128 * (tc + 1), :],
                       waits=([(vS, tail_done[par])] if tc >= 2 else ()), inc=sX[par])
                w_xr = sX[par].at()
                a_r2 = ln_stats(pP[:, ppar], st[:, 0:1], st[:, 1:2],
                                AR32[:, 4096 + ppar.start:4096 + ppar.start + E], st[:, 2:6], [w_pj], psum_src=True)
                kb.emit("vector", lambda e, ppar=ppar, st=st:
                        e.tensor_scalar(l3t[:, 0:E], pP[:, ppar], st[:, 0:1], st[:, 1:2],
                                        ALU.subtract, ALU.mult), waits=[a_r2], inc=[vS, vB])
                kb.emit("vector", lambda e, ppar=ppar:
                        e.tensor_add(x1t[:, ppar], l3t[:, 0:E], xbuf[:, ppar]),
                        waits=[w_xr, (sTS[par], sTS[par].n)], inc=vS)
                kb.drain_inc("vector", vF)
                w_x1 = vF.at()
                kb.dma("scalar", x1_d[s * S + 128 * tc: s * S + 128 * (tc + 1), :],
                       x1t[:, ppar], waits=[w_x1], inc=sTS[par])
                a_r3 = ln_stats(x1t[:, ppar], st[:, 8:9], st[:, 9:10],
                                AR32[:, 4096 + ppar.start:4096 + ppar.start + E], st[:, 10:14], [w_x1])
                kb.emit("vector", lambda e, ppar=ppar, st=st:
                        e.tensor_scalar(l3t[:, 0:E], x1t[:, ppar], st[:, 8:9], st[:, 9:10],
                                        ALU.subtract, ALU.mult), waits=[a_r3], inc=vS)
                kb.drain_inc("vector", vF)
                w_l3 = vF.at()
                kb.emit("scalar", lambda e, ppar=ppar, st=st:
                        e.activation(AR32[:, 4096 + ppar.start:4096 + ppar.start + E],
                                     l3t[:, 0:E], AF.Square,
                                     accum_out=st[:, 14:15]), waits=[w_l3])
                kb.drain_inc("scalar", aF)
                a_ms = aF.at()
                kb.emit("vector", lambda e, st=st:
                        e.tensor_scalar(st[:, 15:16], st[:, 14:15], 1.0 / E, 1e-5,
                                        ALU.mult, ALU.add), waits=[a_ms], inc=vS)
                kb.drain_inc("vector", vF)
                kb.emit("scalar", lambda e, st=st:
                        e.activation(st[:, 15:16], st[:, 15:16], AF.Ln), waits=[vF.at()])
                kb.dr("scalar")
                kb.emit("scalar", lambda e, st=st:
                        e.activation(st[:, 15:16], st[:, 15:16], AF.Exp, scale=-0.5))
                kb.drain_inc("scalar", aF)
                a_rms = aF.at()
                kb.emit("vector", lambda e, ppar=ppar, st=st:
                        e.scalar_tensor_tensor(tff[:, 0:E], l3t[:, 0:E], st[:, 15:16],
                                               scale_bc[:], ALU.mult, ALU.mult),
                        waits=[a_rms], inc=vS)
                kb.emit("vector", lambda e, ppar=ppar:
                        e.tensor_copy(tfh[:, ppar], tff[:, 0:E]),
                        waits=[(sTS[par], sTS[par].n)], inc=vS)
                kb.emit("vector", lambda e, ppar=ppar:
                        e.tensor_tensor(tfl[:, ppar], tff[:, 0:E], tfh[:, ppar], ALU.subtract),
                        inc=vS)
                kb.drain_inc("vector", vF)
                w_tf = vF.at()
                kb.dma("scalar", tf_d[s * S + 128 * tc: s * S + 128 * (tc + 1), :],
                       tfh[:, ppar], waits=[w_tf], inc=sTS[par])
                kb.dma("scalar", tflo_d[s * S + 128 * tc: s * S + 128 * (tc + 1), :],
                       tfl[:, ppar], inc=sTS[par])
                tail_done[par] = vS.n
        _psF.close()
        kb.barrier()
        if _PL == 16: break

    if _PL <= 2 or (11 <= _PL <= 16):
        _finish(); return nc
    _psSEQ.close()
    emit_casts(NEXP)

    _psG = _ps = ExitStack(); _pss.append(_ps)
    tft = _ps.enter_context(nc.sbuf_tensor(_un("tft"), [128, 2 * 8 * 512], BF16))
    tftl = _ps.enter_context(nc.sbuf_tensor(_un("tftl"), [128, 2 * 8 * 512], BF16))
    gsb = _ps.enter_context(nc.sbuf_tensor(_un("gsb"), [128, 2 * 32], F32))
    gA = _ps.enter_context(nc.sbuf_tensor(_un("gA"), [128, 2 * 32], F32))
    gAb = _ps.enter_context(nc.sbuf_tensor(_un("gAb"), [128, 2 * 32], BF16))
    gcmb = _ps.enter_context(nc.sbuf_tensor(_un("gcmb"), [128, 2 * 32], F32))
    gpos = _ps.enter_context(nc.sbuf_tensor(_un("gpos"), [128, 2 * 32], F32))
    gt8 = _ps.enter_context(nc.sbuf_tensor(_un("gt8"), [128, 2 * 8], F32))
    gscr = _ps.enter_context(nc.sbuf_tensor(_un("gscr"), [128, 2 * 32], F32))
    goh = _ps.enter_context(nc.sbuf_tensor(_un("goh"), [128, 2 * 32], F32))
    gwk = _ps.enter_context(nc.sbuf_tensor(_un("gwk"), [128, 2], F32))
    gnar = _ps.enter_context(nc.sbuf_tensor(_un("gnar"), [128, 2 * 16], F32))
    gslot = _ps.enter_context(nc.sbuf_tensor(_un("gslot"), [128, 2 * 4], I32))
    srow = _ps.enter_context(nc.sbuf_tensor(_un("srow"), [1, 3 * 32], F32))
    srhi = _ps.enter_context(nc.sbuf_tensor(_un("srhi"), [1, 32], BF16))
    srlo = _ps.enter_context(nc.sbuf_tensor(_un("srlo"), [1, 32], BF16))
    if True:
        pG = [PA[:, 0:32], PA[:, 512:544]]
        pPos = [PA[:, 1024:1056], PA[:, 1536:1568]]
        pS32 = PB[0:1, 0:32]
        kb.emit("vector", lambda e: e.memset(srhi[:], 0.0), inc=vS)
        kb.emit("vector", lambda e: e.memset(srlo[:], 0.0), inc=vS)
        kb.drain_inc("vector", vF)
        s_ready = vF.at()
        tft_free = [0, 0]
        tile = 0
        for ch in range(8):
            cpar = ch % 2
            for hc in range(8):
                kb.dma("sync", tft[:, cpar * 8 * 512 + hc * 512: cpar * 8 * 512 + (hc + 1) * 512],
                       tf_d[512 * ch:512 * (ch + 1), 128 * hc:128 * (hc + 1)],
                       waits=([(tF, tft_free[cpar])] if (ch >= 2 and hc == 0) else ()),
                       inc=sTFT[cpar], transpose=True)
                kb.dma("sync", tftl[:, cpar * 8 * 512 + hc * 512: cpar * 8 * 512 + (hc + 1) * 512],
                       tflo_d[512 * ch:512 * (ch + 1), 128 * hc:128 * (hc + 1)],
                       inc=sTFT[cpar], transpose=True)
            w_tft = sTFT[cpar].at()
            for tc4 in range(4):
                par = tile % 2
                gsl = slice(par * 32, par * 32 + 32)
                for hc in range(8):
                    base = cpar * 8 * 512 + hc * 512 + 128 * tc4
                    waits = []
                    if hc == 0:
                        waits = [w_tft]
                        if tile >= 2:
                            waits.append((aB, aB.n))
                    kb.emit("tensor", lambda e, base=base, par=par, hc=hc:
                            e.matmul(pG[par], tft[:, base:base + 128],
                                     gwT_hi[:, hc * 32:(hc + 1) * 32],
                                     start=(hc == 0), stop=False), waits=waits)
                    kb.emit("tensor", lambda e, base=base, par=par, hc=hc:
                            e.matmul(pG[par], tftl[:, base:base + 128],
                                     gwT_hi[:, hc * 32:(hc + 1) * 32],
                                     start=False, stop=False))
                    kb.emit("tensor", lambda e, base=base, par=par, hc=hc:
                            e.matmul(pG[par], tft[:, base:base + 128],
                                     gwT_lo[:, hc * 32:(hc + 1) * 32],
                                     start=False, stop=False))
                kb.emit("tensor", lambda e, par=par:
                        e.matmul(pG[par], ones_r[:], gbhi_row[:], start=False, stop=False))
                kb.emit("tensor", lambda e, par=par:
                        e.matmul(pG[par], ones_r[:], gblo_row[:], start=False, stop=True),
                        inc=tF)
                w_g = tF.at()
                tft_free[cpar] = tF.n
                kb.emit("scalar", lambda e, gsl=gsl, par=par:
                        e.copy(gsb[:, gsl], pG[par]), waits=[w_g, (vS, vS.n)])
                kb.drain_inc("scalar", [aF, aB])
                w_ge = aF.at()
                t8v = gt8[:, par * 8:(par + 1) * 8]
                na = gnar[:, par * 16:par * 16 + 16]
                kb.emit("vector", lambda e, gsl=gsl, t8v=t8v: e.max(out=t8v, in_=gsb[:, gsl]),
                        waits=[w_ge], inc=vS)
                kb.drain_inc("vector", vF)
                kb.emit("vector", lambda e, gsl=gsl, t8v=t8v:
                        e.tensor_scalar(gA[:, gsl], gsb[:, gsl], t8v[:, 3:4], None, ALU.is_ge),
                        waits=[vF.at()], inc=vS)
                kb.emit("vector", lambda e, gsl=gsl, na=na:
                        e.tensor_reduce(na[:, 0:1], gsb[:, gsl], AX.X, ALU.max, negate=True),
                        inc=vS)
                kb.drain_inc("vector", vF)
                kb.emit("scalar", lambda e, gsl=gsl, na=na:
                        e.activation(gscr[:, gsl], gsb[:, gsl], AF.Exp, bias=na[:, 0:1]),
                        waits=[vF.at()])
                kb.drain_inc("scalar", aF)
                kb.emit("vector", lambda e, gsl=gsl:
                        e.tensor_mul(gscr[:, gsl], gscr[:, gsl], gA[:, gsl]),
                        waits=[aF.at()], inc=vS)
                kb.dr("vector")
                kb.emit("vector", lambda e, gsl=gsl, na=na:
                        e.tensor_reduce(na[:, 1:2], gscr[:, gsl], AX.X, ALU.add), inc=vS)
                kb.drain_inc("vector", vF)
                kb.emit("vector", lambda e, na=na: e.reciprocal(na[:, 2:3], na[:, 1:2]),
                        waits=[vF.at()], inc=vS)
                kb.drain_inc("vector", vF)
                kb.emit("vector", lambda e, gsl=gsl, na=na:
                        e.tensor_scalar(gcmb[:, gsl], gscr[:, gsl], na[:, 2:3], None, ALU.mult),
                        waits=[vF.at()], inc=vS)
                kb.emit("vector", lambda e, gsl=gsl: e.tensor_copy(gAb[:, gsl], gA[:, gsl]),
                        inc=vS)
                kb.drain_inc("vector", vF)
                w_A = vF.at()
                pos_waits = [w_A, s_ready]
                if tile >= 2:
                    pos_waits.append((aB, aB.n))
                kb.emit("tensor", lambda e, gsl=gsl, par=par:
                        e.matmul(pPos[par], triu[:], gAb[:, gsl], start=True, stop=False),
                        waits=pos_waits)
                kb.emit("tensor", lambda e, par=par:
                        e.matmul(pPos[par], ones_r[:], srhi[:], start=False, stop=False))
                kb.emit("tensor", lambda e, par=par:
                        e.matmul(pPos[par], ones_r[:], srlo[:], start=False, stop=True))
                kb.emit("tensor", lambda e, gsl=gsl, tile=tile:
                        e.matmul(pS32, ones_c[:], gAb[:, gsl],
                                 start=(tile == 0), stop=(tile == NTILE - 1),
                                 skip_group_check=True), inc=tF)
                w_pos = tF.at()
                kb.emit("scalar", lambda e, gsl=gsl, par=par:
                        e.copy(gpos[:, gsl], pPos[par]), waits=[w_pos])
                kb.emit("scalar", lambda e: e.copy(srow[:, 0:32], pS32))
                kb.drain_inc("scalar", [aF, aB])
                w_pe = aF.at()
                kb.emit("vector", lambda e: e.tensor_copy(srhi[:], srow[:, 0:32]),
                        waits=[w_pe], inc=vS)
                kb.dr("vector")
                kb.emit("vector", lambda e: e.tensor_tensor(srow[:, 32:64], srow[:, 0:32],
                                                            srhi[:], ALU.subtract), inc=vS)
                kb.dr("vector")
                kb.emit("vector", lambda e: e.tensor_copy(srlo[:], srow[:, 32:64]), inc=vS)
                kb.drain_inc("vector", vF)
                s_ready = vF.at()
                for k in range(4):
                    kb.emit("vector", lambda e, gsl=gsl:
                            e.scalar_tensor_tensor(gscr[:, gsl], gA[:, gsl], -64.0,
                                                   iota64[:], ALU.mult, ALU.add),
                            waits=[w_pe, (dP, dP.n)], inc=vS)
                    kb.dr("vector")
                    kb.emit("vector", lambda e, gsl=gsl, na=na:
                            e.tensor_reduce(na[:, 4:5], gscr[:, gsl], AX.X, ALU.min), inc=vS)
                    kb.drain_inc("vector", vF)
                    kb.emit("vector", lambda e, gsl=gsl, na=na:
                            e.tensor_scalar(goh[:, gsl], iota32[:], na[:, 4:5], None,
                                            ALU.is_equal), waits=[vF.at()], inc=vS)
                    kb.dr("vector")
                    kb.emit("vector", lambda e, gsl=gsl:
                            e.tensor_mul(gscr[:, gsl], goh[:, gsl], gcmb[:, gsl]), inc=vS)
                    kb.emit("vector", lambda e, gsl=gsl:
                            e.tensor_sub(gA[:, gsl], gA[:, gsl], goh[:, gsl]), inc=vS)
                    kb.dr("vector")
                    kb.emit("vector", lambda e, gsl=gsl, na=na:
                            e.tensor_reduce(na[:, 5:6], gscr[:, gsl], AX.X, ALU.add), inc=vS)
                    kb.emit("vector", lambda e, gsl=gsl:
                            e.tensor_mul(gscr[:, gsl], goh[:, gsl], gpos[:, gsl]), inc=vS)
                    kb.dr("vector")
                    kb.emit("vector", lambda e, gsl=gsl, na=na:
                            e.tensor_reduce(na[:, 6:7], gscr[:, gsl], AX.X, ALU.add), inc=vS)
                    kb.drain_inc("vector", vF)
                    kb.emit("vector", lambda e, na=na:
                            e.tensor_scalar_min(na[:, 7:8], na[:, 6:7], float(CAP - 1)),
                            waits=[vF.at()], inc=vS)
                    kb.dr("vector")
                    kb.emit("vector", lambda e, na=na:
                            e.scalar_tensor_tensor(na[:, 6:7], na[:, 4:5], float(CAP),
                                                   na[:, 7:8], ALU.mult, ALU.add), inc=vS)
                    kb.dr("vector")
                    kb.emit("vector", lambda e, par=par, k=k, na=na:
                            e.tensor_copy(gslot[:, par * 4 + k:par * 4 + k + 1], na[:, 6:7]),
                            inc=vS)
                    kb.emit("vector", lambda e, par=par, na=na:
                            e.tensor_copy(gwk[:, par:par + 1], na[:, 5:6]), inc=vS)
                    kb.drain_inc("vector", vF)
                    w_sl = vF.at()
                    slot_ap = gslot[:, par * 4 + k:par * 4 + k + 1]
                    kb.idma(toki_d[:, :], IndirectOffsetOnAxis(ap=slot_ap, axis=0),
                            tokids[:, tile:tile + 1], None,
                            waits=[w_sl, prefill_done], inc=dP)
                    kb.idma(comb_d[:, :], IndirectOffsetOnAxis(ap=slot_ap, axis=0),
                            gwk[:, par:par + 1], None, inc=dP)
                    kb.dma("scalar", slots_d[tile, k, :], slot_ap, waits=[w_sl], inc=sSD)
                tile += 1
        slots_stored = sSD.at()
    _psG.close()
    kb.barrier()

    if _PL <= 3:
        _finish(); return nc
    _psH = _ps = ExitStack(); _pss.append(_ps)
    gidx = _ps.enter_context(nc.sbuf_tensor(_un("gidx"), [128, NTC], I32))
    gbuf = _ps.enter_context(nc.sbuf_tensor(_un("gbuf"), [128, 2 * E], BF16))
    if True:
        for e_ in range(NEXP):
            w_prev = dP.at()
            for c in range(NTC):
                kb.dma("gpsimd", gidx[:, c:c + 1],
                       toki_d[e_ * CAP + 128 * c: e_ * CAP + 128 * (c + 1), :],
                       waits=([w_prev, sPK[0].at(), sPK[1].at()] if c == 0 else ()), inc=dP)
            w_gi = dP.at()
            for c in range(NTC):
                gp = (e_ * NTC + c) % 2
                kb.idma(gbuf[:, gp * E:(gp + 1) * E], None,
                        tf_d[:, :], IndirectOffsetOnAxis(ap=gidx[:, c:c + 1], axis=0),
                        waits=([w_gi] if c == 0 else []) + [sPK[gp].at()], inc=dP)
                w_gc = dP.at()
                kb.dma("gpsimd", packed_d[e_ * CAP + 128 * c: e_ * CAP + 128 * (c + 1), :],
                       gbuf[:, gp * E:(gp + 1) * E], waits=[w_gc], inc=sPK[gp])
    _psH.close()
    kb.barrier()

    if _PL <= 4:
        _finish(); return nc
    w1T = ARENA[:, 0:32768]
    w2T = ARENA[:, 32768:49152]
    tfpT = ARENA[:, 49152:49152 + 8 * CAP]
    _psI = _ps = ExitStack(); _pss.append(_ps)
    b1r = _ps.enter_context(nc.sbuf_tensor(_un("b1r"), [1, 2 * F2], BF16))
    b2r = _ps.enter_context(nc.sbuf_tensor(_un("b2r"), [1, 2 * E], BF16))
    cmbrb = _ps.enter_context(nc.sbuf_tensor(_un("cmbrb"), [1, 2 * CAP], BF16))
    cmbc = _ps.enter_context(nc.sbuf_tensor(_un("cmbc"), [128, 2 * NTC], F32))
    swg = _ps.enter_context(nc.sbuf_tensor(_un("swg"), [128, E], F32))
    swl = _ps.enter_context(nc.sbuf_tensor(_un("swl"), [128, E], F32))
    swl2 = swl
    sws = _ps.enter_context(nc.sbuf_tensor(_un("sws"), [128, E], F32))
    swt = _ps.enter_context(nc.sbuf_tensor(_un("swt"), [128, E], F32))
    hw = _ps.enter_context(nc.sbuf_tensor(_un("hw"), [128, 2 * E], BF16))
    hwTs = _ps.enter_context(nc.sbuf_tensor(_un("hwT"), [128, 2 * E], BF16))
    osb = _ps.enter_context(nc.sbuf_tensor(_un("osb"), [128, 2 * E], F32))
    if True:
        pH1 = PA
        pW2 = PB[:, 0:E]
        pHwT = [PB16[:, 2048:3072], PB16[:, 3072:4096]]
        wload_done = {}
        cmb_done = {}
        def load_expert(e_, epar):
            waits = [sCAST.at(), (tF, tF.n), (vS, vS.n)]
            for hc in range(8):
                kb.dma("sync", w1T[:, epar * 8 * F2 + hc * F2: epar * 8 * F2 + (hc + 1) * F2],
                       w1_bf[e_][:, 128 * hc:128 * (hc + 1)],
                       waits=(waits if hc == 0 else ()), inc=sWE[epar], transpose=True)
            for fc in range(8):
                kb.dma("sync", w2T[:, epar * 8 * E + fc * E: epar * 8 * E + (fc + 1) * E],
                       w2_bf[e_][:, 128 * fc:128 * (fc + 1)], inc=sWE[epar], transpose=True)
            pass
            kb.dma("sync", b1r[:, epar * F2:(epar + 1) * F2], b1_bf[e_][None, :], inc=sWE[epar])
            kb.dma("sync", b2r[:, epar * E:(epar + 1) * E], b2_bf[e_][None, :], inc=sWE[epar])
            wload_done[e_] = sWE[epar].at()
            kb.dma("gpsimd", cmbrb[:, epar * CAP:(epar + 1) * CAP],
                   comb_d.rearrange("a one -> (one a)")[None, e_ * CAP:(e_ + 1) * CAP],
                   waits=[(tF, tF.n)], inc=sCM[epar])
            for c in range(NTC):
                kb.dma("gpsimd", cmbc[:, epar * NTC + c:epar * NTC + c + 1],
                       comb_d[e_ * CAP + 128 * c:e_ * CAP + 128 * (c + 1), :], inc=sCM[epar])
            cmb_done[e_] = sCM[epar].at()

        load_expert(0, 0)
        for e_ in range(NEXP):
            epar = e_ % 2
            if e_ + 1 < NEXP:
                load_expert(e_ + 1, (e_ + 1) % 2)
            w_w = wload_done[e_]
            w_c = cmb_done[e_]
            tfp_waits = [(tF, tF.n)]
            for hc in range(8):
                kb.dma("sync", tfpT[:, hc * CAP:(hc + 1) * CAP],
                       packed_d[e_ * CAP:(e_ + 1) * CAP, 128 * hc:128 * (hc + 1)],
                       waits=(tfp_waits if hc == 0 else ()), inc=sWE[epar], transpose=True)
            w_tfp = sWE[epar].at()
            cmbrb_ok = w_c
            for tc in range(NTC):
                for hc in range(8):
                    waits = []
                    if hc == 0:
                        waits = [w_w, w_tfp, (vB, vB.n)]
                    for fc in range(4):
                        kb.emit("tensor", lambda e, hc=hc, tc=tc, fc=fc, epar=epar:
                                e.matmul(pH1[:, 512 * fc:512 * (fc + 1)],
                                         tfpT[:, hc * CAP + 128 * tc:
                                              hc * CAP + 128 * (tc + 1)],
                                         w1T[:, epar * 8 * F2 + hc * F2 + 512 * fc:
                                             epar * 8 * F2 + hc * F2 + 512 * (fc + 1)],
                                         start=(hc == 0), stop=False),
                                waits=(waits if fc == 0 else ()))
                for fc in range(4):
                    kb.emit("tensor", lambda e, fc=fc, epar=epar:
                            e.matmul(pH1[:, 512 * fc:512 * (fc + 1)], ones_r[:],
                                     b1r[:, epar * F2 + 512 * fc: epar * F2 + 512 * (fc + 1)],
                                     start=False, stop=True),
                            inc=(tF if fc == 3 else None))
                w_h1 = tF.at()
                kb.emit("vector", lambda e:
                        e.tensor_scalar_min(swg[:], pH1[:, 0:F2:2], 7.0),
                        waits=[w_h1, (aF, aF.n)], inc=vS)
                kb.emit("vector", lambda e:
                        e.tensor_scalar(swl[:], pH1[:, 1:F2:2], 7.0, -7.0, ALU.min, ALU.max),
                        waits=[(pS, pS.n)], inc=[vS, vB])
                kb.drain_inc("vector", vF)
                w_gl = vF.at()
                kb.emit("gpsimd", lambda e:
                        e.tensor_scalar(swl2[:], swl[:], INV_A, INV_A, ALU.mult, ALU.add),
                        waits=[w_gl, (vS, vS.n)])
                kb.drain_inc("gpsimd", pS)
                w_l3 = pS.at()
                kb.emit("scalar", lambda e:
                        e.activation(sws[:], swg[:], AF.Silu, scale=1.702),
                        waits=[w_gl, (vS, vS.n)])
                kb.drain_inc("scalar", aF)
                w_si = aF.at()
                hpar = tc % 2
                kb.emit("vector", lambda e:
                        e.tensor_mul(swt[:], sws[:], swl2[:]), waits=[w_si, w_l3], inc=vS)
                kb.emit("vector", lambda e, hpar=hpar, epar=epar, tc=tc:
                        e.tensor_scalar(hw[:, hpar * E:(hpar + 1) * E], swt[:],
                                        cmbc[:, epar * NTC + tc:epar * NTC + tc + 1],
                                        None, ALU.mult),
                        waits=[w_c, (tF, tF.n)], inc=vS)
                kb.drain_inc("vector", vF)
                w_hw = vF.at()
                for j in range(8):
                    waits = []
                    if j == 0:
                        waits = [w_hw, (aB, aB.n)]
                    kb.emit("tensor", lambda e, j=j, hpar=hpar:
                            e.transpose(pHwT[hpar][:, 128 * j:128 * (j + 1)],
                                        hw[:, hpar * E + 128 * j: hpar * E + 128 * (j + 1)],
                                        ident_bf[:]),
                            waits=waits, inc=(tF if j == 7 else None))
                w_ht = tF.at()
                kb.emit("scalar", lambda e, hpar=hpar:
                        e.copy(hwTs[:, hpar * E:(hpar + 1) * E], pHwT[hpar]),
                        waits=[w_ht])
                kb.drain_inc("scalar", [aF, aB])
                w_hte = aF.at()
                for fc in range(8):
                    waits = []
                    if fc == 0:
                        waits = [w_hte, cmbrb_ok, (aB, aB.n)]
                    kb.emit("tensor", lambda e, fc=fc, hpar=hpar, epar=epar:
                            e.matmul(pW2[:, 0:512],
                                     hwTs[:, hpar * E + 128 * fc: hpar * E + 128 * (fc + 1)],
                                     w2T[:, epar * 8 * E + fc * E: epar * 8 * E + fc * E + 512],
                                     start=(fc == 0), stop=False), waits=waits)
                    kb.emit("tensor", lambda e, fc=fc, hpar=hpar, epar=epar:
                            e.matmul(pW2[:, 512:E],
                                     hwTs[:, hpar * E + 128 * fc: hpar * E + 128 * (fc + 1)],
                                     w2T[:, epar * 8 * E + fc * E + 512: epar * 8 * E + (fc + 1) * E],
                                     start=(fc == 0), stop=False))
                kb.emit("tensor", lambda e, tc=tc, epar=epar:
                        e.matmul(pW2[:, 0:512],
                                 cmbrb[:, epar * CAP + 128 * tc: epar * CAP + 128 * (tc + 1)],
                                 b2r[:, epar * E: epar * E + 512], start=False, stop=True))
                kb.emit("tensor", lambda e, tc=tc, epar=epar:
                        e.matmul(pW2[:, 512:E],
                                 cmbrb[:, epar * CAP + 128 * tc: epar * CAP + 128 * (tc + 1)],
                                 b2r[:, epar * E + 512:(epar + 1) * E], start=False, stop=True),
                        inc=tF)
                w_w2 = tF.at()
                opar = tc % 2
                kb.emit("scalar", lambda e, opar=opar:
                        e.copy(osb[:, opar * E:(opar + 1) * E], pW2),
                        waits=[w_w2, (sOP[opar], sOP[opar].n)])
                kb.drain_inc("scalar", [aF, aB])
                w_oe = aF.at()
                kb.dma("scalar", outp_d[e_ * CAP + 128 * tc: e_ * CAP + 128 * (tc + 1), :],
                       osb[:, opar * E:(opar + 1) * E], waits=[w_oe], inc=sOP[tc % 2])
    _psI.close()
    kb.barrier()

    if _PL <= 5:
        _finish(); return nc
    _psJ = _ps = ExitStack(); _pss.append(_ps)
    fidx = _ps.enter_context(nc.sbuf_tensor(_un("fidx"), [128, 2 * 4], I32))
    fst = _ps.enter_context(nc.sbuf_tensor(_un("fst"), [128, 32], F32))
    AR32p5 = ARENA.bitcast(F32)
    fG = AR32p5[:, 0:8 * E]
    fx1 = AR32p5[:, 8 * E:10 * E]
    fl3 = AR32p5[:, 10 * E:12 * E]
    fmo = AR32p5[:, 12 * E:14 * E]
    fo = AR32p5[:, 14 * E:16 * E]
    if True:
        done_par = [0, 0]
        pool_done = [0, 0]
        for i in range(NTILE):
            par = i % 2
            s_, r_ = i // 16, 128 * (i % 16)
            for k in range(4):
                kb.dma("gpsimd", fidx[:, par * 4 + k:par * 4 + k + 1],
                       slots_d[i, k, :][:, None],
                       waits=([(vS, done_par[par]), (pS, pool_done[par]), slots_stored]
                              if k == 0 and i >= 2 else ([slots_stored] if k == 0 else ())),
                       inc=sFG[par])
            w_fi = sFG[par].at()
            for k in range(4):
                kb.idma(fG[:, (par * 4 + k) * E:(par * 4 + k + 1) * E], None,
                        outp_d[:, :],
                        IndirectOffsetOnAxis(ap=fidx[:, par * 4 + k:par * 4 + k + 1], axis=0),
                        waits=([w_fi] if k == 0 else ()), inc=sFG[par])
            w_g4 = sFG[par].at()
            kb.dma("sync", fx1[:, par * E:(par + 1) * E],
                   x1_d[128 * i:128 * (i + 1), :],
                   waits=([(vS, done_par[par])] if i >= 2 else ()), inc=sX[par])
            w_x1 = sX[par].at()
            b0 = (par * 4) * E
            kb.emit("vector", lambda e, b0=b0:
                    e.tensor_add(fG[:, b0:b0 + E], fG[:, b0:b0 + E], fG[:, b0 + E:b0 + 2 * E]),
                    waits=[w_g4], inc=vS)
            kb.emit("gpsimd", lambda e, b0=b0:
                    e.tensor_add(fG[:, b0 + 2 * E:b0 + 3 * E], fG[:, b0 + 2 * E:b0 + 3 * E],
                                 fG[:, b0 + 3 * E:b0 + 4 * E]), waits=[w_g4])
            kb.drain_inc("gpsimd", pS)
            pool_done[par] = pS.n
            kb.emit("vector", lambda e, b0=b0, par=par:
                    e.tensor_add(fmo[:, par * E:(par + 1) * E], fG[:, b0:b0 + E],
                                 fG[:, b0 + 2 * E:b0 + 3 * E]), waits=[pS.at()], inc=vS)
            st = fst[:, par * 16:(par + 1) * 16]
            a_r3 = ln_stats(fx1[:, par * E:(par + 1) * E], st[:, 0:1], st[:, 1:2],
                            AR32p5[:, (16 + par) * E:(17 + par) * E], st[:, 2:6], [w_x1])
            kb.emit("vector", lambda e, par=par, st=st:
                    e.tensor_scalar(fl3[:, par * E:(par + 1) * E], fx1[:, par * E:(par + 1) * E],
                                    st[:, 0:1], st[:, 1:2], ALU.subtract, ALU.mult),
                    waits=[a_r3], inc=vS)
            kb.emit("vector", lambda e, par=par:
                    e.tensor_add(fl3[:, par * E:(par + 1) * E], fl3[:, par * E:(par + 1) * E],
                                 fmo[:, par * E:(par + 1) * E]), inc=vS)
            kb.drain_inc("vector", vF)
            w_f = vF.at()
            a_r4 = ln_stats(fl3[:, par * E:(par + 1) * E], st[:, 6:7], st[:, 7:8],
                            AR32p5[:, (16 + par) * E:(17 + par) * E], st[:, 8:12], [w_f])
            kb.emit("vector", lambda e, par=par, st=st:
                    e.tensor_scalar(fo[:, par * E:(par + 1) * E], fl3[:, par * E:(par + 1) * E],
                                    st[:, 6:7], st[:, 7:8], ALU.subtract, ALU.mult),
                    waits=[a_r4, (sOUT, sOUT.n)], inc=vS)
            kb.emit("vector", lambda e, par=par:
                    e.tensor_add(fo[:, par * E:(par + 1) * E], fo[:, par * E:(par + 1) * E],
                                 fx1[:, par * E:(par + 1) * E]), inc=vS)
            kb.drain_inc("vector", vF)
            w_out = vF.at()
            kb.dma("scalar", out_d[s_, r_:r_ + 128, :], fo[:, par * E:(par + 1) * E],
                   waits=[w_out], inc=sOUT)
            done_par[par] = vS.n

    _psJ.close()
    kb.barrier()
    kb.emit("scalar", None, waits=[sOUT.at()])
    kb.run()
    for _p in _pss: _p.close()
    stack.close()
    return nc

NCORES = 8

def kernel(**inputs):
    full_x = np.ascontiguousarray(inputs["L0_attn_x"])
    consts = build_consts()
    nc = build_program(debug=DEBUG)
    weights = {k: np.asarray(v) for k, v in inputs.items() if k != "L0_attn_x"}
    in_maps = []
    for c in range(NCORES):
        m = {"x_shard": full_x[NSEQ * c:NSEQ * (c + 1)]}
        m.update(weights)
        m.update(consts)
        in_maps.append(m)
    res = run_bass_kernel_spmd(nc, in_maps, core_ids=list(range(NCORES)))
    out = np.concatenate([res.results[c]["out"] for c in range(NCORES)], axis=0)
    return out.astype(np.float32)

if __name__ == "__main__":
    nc = build_program()
    print("instructions:", nc.n_instructions())
